# version 06
# baseline (speedup 1.0000x reference)
import os
import numpy as np
from contextlib import ExitStack

import concourse.bass as bass
import concourse.mybir as mybir
from concourse.bass_utils import run_bass_kernel_spmd

F32 = mybir.dt.float32
I32 = mybir.dt.int32
AF = mybir.ActivationFunctionType
ALU = mybir.AluOpType
AX = mybir.AxisListType

T = 4096
D = 1024
DI = 2048
S = 16
LS = 512
NSC = T // LS
NDB = DI // 128
KD = D // 128
NOB = D // 128
NTT = LS // 128
EPS = 1e-5

W0 = 16 * 40
T0 = 384 + 16


def build_nc(debug_taps=(), nsc=NSC):
    nc = bass.Bass("TRN2", target_bir_lowering=False, detect_race_conditions=False)

    x_hbm = nc.dram_tensor("x", [T, D], F32, kind="ExternalInput").ap()
    w_xp = nc.dram_tensor("x_proj_w", [DI, D], F32, kind="ExternalInput").ap()
    w_dt = nc.dram_tensor("dt_proj_w", [DI, D], F32, kind="ExternalInput").ap()
    b_dt = nc.dram_tensor("dt_proj_b", [DI], F32, kind="ExternalInput").ap()
    a_log = nc.dram_tensor("A_log", [DI, S], F32, kind="ExternalInput").ap()
    d_vec = nc.dram_tensor("D_vec", [DI], F32, kind="ExternalInput").ap()
    w_b = nc.dram_tensor("B_proj_w", [S, D], F32, kind="ExternalInput").ap()
    w_c = nc.dram_tensor("C_proj_w", [S, D], F32, kind="ExternalInput").ap()
    w_cv = nc.dram_tensor("conv_w", [DI, 4], F32, kind="ExternalInput").ap()
    b_cv = nc.dram_tensor("conv_b", [DI], F32, kind="ExternalInput").ap()
    w_op = nc.dram_tensor("out_proj_w", [D, DI], F32, kind="ExternalInput").ap()
    f_win = nc.dram_tensor("ffn_Win", [D, 64], F32, kind="ExternalInput").ap()
    f_bin = nc.dram_tensor("ffn_bin", [64], F32, kind="ExternalInput").ap()
    f_w1 = nc.dram_tensor("ffn_W1", [64, 16], F32, kind="ExternalInput").ap()
    f_b1 = nc.dram_tensor("ffn_b1", [64, 16], F32, kind="ExternalInput").ap()
    f_w2 = nc.dram_tensor("ffn_W2", [64, 16], F32, kind="ExternalInput").ap()
    f_b2 = nc.dram_tensor("ffn_b2", [64], F32, kind="ExternalInput").ap()
    f_wo = nc.dram_tensor("ffn_Wout", [64, D], F32, kind="ExternalInput").ap()
    f_bo = nc.dram_tensor("ffn_bout", [D], F32, kind="ExternalInput").ap()
    out_hbm = nc.dram_tensor("out", [T, D], F32, kind="ExternalOutput").ap()

    scr_xp = nc.dram_tensor("scr_xp", [NDB, 128, KD * 128], F32, kind="Internal").ap()
    scr_dt = nc.dram_tensor("scr_dt", [NDB, 128, KD * 128], F32, kind="Internal").ap()
    scr_op = nc.dram_tensor("scr_op", [NOB, 128, NDB * 128], F32, kind="Internal").ap()

    taps = {}
    for name, shape in [("xn_fm", [128, KD * LS]), ("dt", [128, NDB * LS]),
                        ("dtx", [128, NDB * LS]), ("y", [128, NDB * LS]),
                        ("bbc", [128, LS]), ("state", [128, 256]),
                        ("ident", [128, 128]), ("es", [16, 16 * 128]),
                        ("xtok", [128, NTT * 1024]), ("mrow", [128, LS]),
                        ("m2row", [128, LS])]:
        if name in debug_taps:
            taps[name] = nc.dram_tensor("dbg_" + name, shape, F32, kind="ExternalOutput").ap()

    with ExitStack() as stack:
        def sb(name, shape, dt=F32):
            return stack.enter_context(nc.sbuf_tensor(name, shape, dt))

        def ps(name, shape, dt=F32):
            return stack.enter_context(nc.psum_tensor(name, shape, dt))

        x_tok = sb("x_tok", [128, NTT * 1024])
        out_stg = [sb(f"out_stg{i}", [128, 128]) for i in range(2)]
        x2s = sb("x2s", [128, LS])
        mneg_bc = sb("mneg_bc", [128, LS])
        r_bc = sb("r_bc", [128, LS])
        ones_f = sb("ones_f", [128, 128])
        xn_fm = sb("xn_fm", [128, KD * LS])
        dt_sup = sb("dt_sup", [128, NDB * LS])
        dtx_sup = sb("dtx_sup", [128, NDB * LS])
        y_sup = sb("y_sup", [128, NDB * LS])
        wpool = sb("wpool", [128, 2 * 1024])
        g_fm = [sb(f"g_fm{i}", [64, LS]) for i in range(2)]
        bc_cmp = sb("bc_cmp", [16, 2 * LS])
        b_bc = [sb(f"b_bc{i}", [128, LS]) for i in range(2)]
        c_bc = [sb(f"c_bc{i}", [128, LS]) for i in range(2)]
        a_buf = [sb(f"a_buf{i}", [128, LS]) for i in range(2)]
        h_buf = [sb(f"h_buf{i}", [128, LS]) for i in range(2)]
        uz_buf = sb("uz_buf", [128, LS])
        xc_buf = [sb(f"xc_buf{i}", [128, LS]) for i in range(2)]
        xi_buf = [sb(f"xi_buf{i}", [128, LS + 3]) for i in range(2)]
        prod = sb("prod", [128, LS])
        g_tile = [sb(f"g_tile{i}", [128, LS]) for i in range(2)]
        h_i = [sb(f"h_i{i}", [64, LS]) for i in range(2)]
        state = sb("state", [128, 256])
        a_cols = sb("a_cols", [128, NDB * S])
        dtb_c = sb("dtb_c", [128, NDB])
        d_c = sb("d_c", [128, NDB])
        cvb_c = sb("cvb_c", [128, NDB])
        cvw_c = sb("cvw_c", [128, NDB * 4])
        tails = sb("tails", [128, NDB * 3])
        ident = sb("ident", [128, 128])
        es_all = sb("es_all", [16, S * 128])
        wbc_sb = sb("wbc_sb", [128, 2 * KD * 16])
        ffn_win = sb("ffn_win", [128, KD * 64])
        ffn_wo = sb("ffn_wo", [64, D])
        ffn_w1 = sb("ffn_w1", [64, 16])
        ffn_b2c = sb("ffn_b2c", [64, 16])
        ffn_w2 = sb("ffn_w2", [64, 16])
        ffn_b2s = sb("ffn_b2s", [64, 1])
        ffn_binc = sb("ffn_binc", [64, 1])
        ffn_boc = sb("ffn_boc", [128, NOB])
        eps_c = sb("eps_c", [128, 1])
        zcol = sb("zcol", [128, 1])

        pt_tr = ps("pt_tr", [128, 512])
        pt_xi = ps("pt_xi", [128, 512])
        pt_dt = ps("pt_dt", [128, 512])
        pt_bcb = ps("pt_bcb", [128, 512])
        pt_bcc = ps("pt_bcc", [128, 512])
        pt_tr2 = ps("pt_tr2", [128, 512])
        pt_out = [ps(f"pt_out{i}", [128, 512]) for i in range(2)]

        def tr_bank(g):
            return pt_tr if g % 2 == 0 else pt_tr2

        def pe_fence(tensor, bank, sem, k=1):
            tensor.matmul(bank[0:1, 0:1], lhsT=zcol[:], rhs=zcol[:],
                          start=False, stop=True,
                          skip_group_check=True).then_inc(sem, k)

        def sm(name):
            return stack.enter_context(nc.semaphore(name))

        s_x = sm("s_x"); s_w = sm("s_w"); s_wfree = sm("s_wfree")
        s_scr = sm("s_scr"); s_ld0 = sm("s_ld0"); s_const = sm("s_const")
        s_gpc = sm("s_gpc")
        s_lnv = sm("s_lnv"); s_lnm = sm("s_lnm"); s_lnz = sm("s_lnz")
        s_lns = sm("s_lns"); s_lnx = sm("s_lnx")
        s_tr = sm("s_tr"); s_trc = sm("s_trc")
        s_evxi = sm("s_evxi"); s_evdt = sm("s_evdt"); s_conv = sm("s_conv")
        s_bcmp = sm("s_bcmp"); s_evsm = sm("s_evsm"); s_ffnmm = sm("s_ffnmm")
        s_hi = sm("s_hi"); s_gch = sm("s_gch"); s_xc = sm("s_xc"); s_dtx = sm("s_dtx")
        s_bcmm = sm("s_bcmm"); s_evbc = sm("s_evbc")
        s_a = sm("s_a"); s_scan = sm("s_scan")
        s_fpre = sm("s_fpre"); s_gate = sm("s_gate"); s_prod = sm("s_prod")
        s_flt = sm("s_flt")
        s_sq = sm("s_sq"); s_u1 = sm("s_u1"); s_u2 = sm("s_u2")
        s_st = sm("s_st"); s_st3 = sm("s_st3"); s_lnb = sm("s_lnb")
        s_u2r = sm("s_u2r")
        s_ost = sm("s_ost")

        block = stack.enter_context(nc.Block())

        @block.sync
        def _(sync):
            loads = [
                (a_cols[:].rearrange("p (a s) -> p a s", s=16),
                 a_log.rearrange("(a p) s -> p a s", p=128)),
                (dtb_c[:], b_dt.rearrange("(a p) -> p a", p=128)),
                (d_c[:], d_vec.rearrange("(a p) -> p a", p=128)),
                (cvb_c[:], b_cv.rearrange("(a p) -> p a", p=128)),
                (cvw_c[:].rearrange("p (a k) -> p a k", k=4),
                 w_cv.rearrange("(a p) k -> p a k", p=128)),
                (ffn_win[:].rearrange("p (a m) -> p a m", m=64),
                 f_win.rearrange("(a p) m -> p a m", p=128)),
                (ffn_wo[:], f_wo),
                (ffn_w1[:], f_w1),
                (ffn_b2c[:], f_b1),
                (ffn_w2[:], f_w2),
                (ffn_b2s[:], f_b2.rearrange("(m o) -> m o", o=1)),
                (ffn_binc[:], f_bin.rearrange("(m o) -> m o", o=1)),
                (ffn_boc[:], f_bo.rearrange("(a p) -> p a", p=128)),
                (dt_sup[0:16, 4096:5120], w_b),
                (dt_sup[0:16, 5120:6144], w_c),
            ]
            for dst, src in loads:
                sync.dma_start(out=dst, in_=src,
                               allow_slow_non_contiguous=True).then_inc(s_ld0, 16)
        NLD0 = 15 * 16

        @block.gpsimd
        def _(gpsimd):
            ii = dt_sup[:, 0:128].bitcast(I32)
            ic = dt_sup[:, 256:257].bitcast(I32)
            gpsimd.iota(ii, pattern=[[1, 128]], base=0, channel_multiplier=0)
            gpsimd.iota(ic, pattern=[[1, 1]], base=0, channel_multiplier=1)
            gpsimd.tensor_copy(out=ident[:], in_=ii)
            gpsimd.tensor_copy(out=dtx_sup[:, 0:1], in_=ic)
            gpsimd.tensor_scalar(out=ident[:], in0=ident[:], scalar1=dtx_sup[:, 0:1],
                                 scalar2=None, op0=ALU.is_equal)
            gpsimd.memset(eps_c[:], EPS)
            gpsimd.memset(zcol[:], 0.0)
            gpsimd.memset(ones_f[:], 1.0)
            gpsimd.memset(state[:], 0.0)
            gpsimd.memset(tails[:], 0.0).then_inc(s_gpc, 1)

        @block.scalar
        def _(scalar):
            scalar.wait_ge(s_ld0, NLD0)
            scalar.activation(out=a_cols[:], in_=a_cols[:], func=AF.Exp)
            scalar.activation(out=a_cols[:], in_=a_cols[:],
                              func=AF.Copy, scale=-1.0).then_inc(s_const, 1)

        @block.vector
        def _(vector):
            vector.wait_ge(s_ld0, NLD0)
            vector.wait_ge(s_gpc, 1)
            vector.tensor_copy(out=es_all[:].rearrange("p (s m) -> p s m", m=128),
                               in_=ident[0:16, 0:16].unsqueeze(2).broadcast_to([16, 16, 128]))
            vector.scalar_tensor_tensor(out=ffn_b2c[:], in0=ffn_w1[:],
                                        scalar=ffn_binc[:], in1=ffn_b2c[:],
                                        op0=ALU.mult, op1=ALU.add).then_inc(s_const, 1)

        pre_plan = []
        for blk in range(NDB):
            pre_plan.append((w_xp[blk * 128:(blk + 1) * 128, :], scr_xp[blk], KD))
        for blk in range(NDB):
            pre_plan.append((w_dt[blk * 128:(blk + 1) * 128, :], scr_dt[blk], KD))
        for blk in range(NOB):
            pre_plan.append((w_op[blk * 128:(blk + 1) * 128, :], scr_op[blk], NDB))
        NPRE = sum(p[2] for p in pre_plan)

        def pre_wslot(i, nkt):
            if nkt <= 8:
                return wpool[:, (i % 2) * 1024: (i % 2) * 1024 + nkt * 128]
            return dt_sup[:, 6144: 6144 + nkt * 128]

        @block.sync
        def _(sync):
            for i, (srcap, dst, nkt) in enumerate(pre_plan):
                if nkt <= 8:
                    sync.wait_ge(s_scr, max(0, i - 1) * 16)
                else:
                    sync.wait_ge(s_scr, i * 16)
                sync.dma_start(out=pre_wslot(i, nkt), in_=srcap).then_inc(s_w, 16)
                stage = xn_fm[:, (i % 2) * 2048: (i % 2) * 2048 + nkt * 128]
                trc_after = sum(p[2] for p in pre_plan[:i + 1])
                sync.wait_ge(s_trc, trc_after)
                sync.dma_start(out=dst, in_=stage).then_inc(s_scr, 16)

        @block.tensor
        def _(tensor):
            tensor.wait_ge(s_const, 2)
            g = 0
            for i, (srcap, dst, nkt) in enumerate(pre_plan):
                wslot = pre_wslot(i, nkt)
                tensor.wait_ge(s_w, (i + 1) * 16)
                for kt in range(nkt):
                    tensor.wait_ge(s_trc, max(0, g - 1))
                    tensor.transpose(tr_bank(g)[:, 0:128],
                                     in_=wslot[:, kt * 128:(kt + 1) * 128],
                                     identity=ident[:])
                    pe_fence(tensor, tr_bank(g), s_tr)
                    g += 1
            tensor.wait_ge(s_ld0, NLD0)
            for j in range(2 * KD):
                g = NPRE + j
                tensor.wait_ge(s_trc, g - 1)
                tensor.transpose(tr_bank(g)[:, 0:16],
                                 in_=dt_sup[0:16, 4096 + j * 128: 4096 + (j + 1) * 128],
                                 identity=ident[0:16, 0:16])
                pe_fence(tensor, tr_bank(g), s_tr)

        @block.scalar
        def _(scalar):
            g = 0
            for i, (srcap, dst, nkt) in enumerate(pre_plan):
                stage = xn_fm[:, (i % 2) * 2048: (i % 2) * 2048 + nkt * 128]
                for kt in range(nkt):
                    scalar.wait_ge(s_tr, g + 1)
                    scalar.copy(out=stage[:, kt * 128:(kt + 1) * 128],
                                in_=tr_bank(g)[:, 0:128]).then_inc(s_trc, 1)
                    g += 1
            for j in range(2 * KD):
                g = NPRE + j
                scalar.wait_ge(s_tr, g + 1)
                scalar.copy(out=wbc_sb[:, j * 16:(j + 1) * 16],
                            in_=tr_bank(g)[:, 0:16]).then_inc(s_trc, 1)

        @block.sync
        def _(sync):
            for tt in range(NTT):
                sync.dma_start(out=x_tok[:, tt * 1024:(tt + 1) * 1024],
                               in_=x_hbm[tt * 128:(tt + 1) * 128, :]).then_inc(s_x, 16)

        def trc_xn_end(c):
            return T0 + 64 * c + 32

        def trc_all_end(c):
            return T0 + 64 * (c + 1)

        def tr_xn_tile_end(gt):
            c_, tt_ = gt // 4, gt % 4
            return T0 + 64 * c_ + 8 * (tt_ + 1)

        def tr_prod_end(j):
            c_, ob_ = j // 8, j % 8
            return T0 + 64 * c_ + 32 + 4 * (ob_ + 1)

        for c in range(nsc):
            t0 = c * LS

            @block.sync
            def _(sync, c=c):
                srcs = []
                for dblk in range(NDB):
                    srcs.append(scr_xp[dblk])
                    srcs.append(scr_dt[dblk])
                for ob in range(NOB):
                    srcs.append(scr_op[ob][:, 0:1024])
                    srcs.append(scr_op[ob][:, 1024:2048])
                for u, sap in enumerate(srcs):
                    gs = 48 * c + u
                    if gs < 2:
                        sync.wait_ge(s_trc, T0)
                    else:
                        sync.wait_ge(s_wfree, gs - 1)
                    slot = wpool[:, (gs % 2) * 1024:(gs % 2 + 1) * 1024]
                    sync.dma_start(out=slot, in_=sap).then_inc(s_w, 16)

            @block.tensor
            def _(tensor, c=c):
                g = T0 + 64 * c
                for tt in range(NTT):
                    tensor.wait_ge(s_x, 16 * (4 * c + tt + 1))
                    for kt in range(KD):
                        tensor.wait_ge(s_trc, g - 1)
                        tensor.transpose(tr_bank(g)[:, 0:128],
                                         in_=x_tok[:, tt * 1024 + kt * 128: tt * 1024 + (kt + 1) * 128],
                                         identity=ident[:])
                        pe_fence(tensor, tr_bank(g), s_tr)
                        g += 1
                tensor.wait_ge(s_gate, 8 * c)
                tensor.wait_ge(s_evxi, 16 * c)
                tensor.wait_ge(s_evdt, 16 * c)
                tensor.wait_ge(s_trc, trc_xn_end(c))
                for kt in range(KD):
                    tensor.matmul(pt_xi[:], lhsT=ones_f[:],
                                  rhs=xn_fm[:, kt * LS:(kt + 1) * LS],
                                  start=(kt == 0), stop=(kt == KD - 1))
                pe_fence(tensor, pt_xi, s_u1)
                for kt in range(KD):
                    tensor.wait_ge(s_sq, 8 * c + kt + 1)
                    tensor.matmul(pt_dt[:], lhsT=ones_f[:],
                                  rhs=x2s[:],
                                  start=(kt == 0),
                                  stop=(kt == KD - 1)).then_inc(s_u2r, 1)
                pe_fence(tensor, pt_dt, s_u2)

            @block.gpsimd
            def _(gpsimd, c=c):
                gpsimd.wait_ge(s_trc, trc_xn_end(c))
                for kt in range(KD):
                    if kt >= 1:
                        gpsimd.wait_ge(s_u2r, 8 * c + kt)
                    gpsimd.tensor_tensor(out=x2s[:],
                                         in0=xn_fm[:, kt * LS:(kt + 1) * LS],
                                         in1=xn_fm[:, kt * LS:(kt + 1) * LS],
                                         op=ALU.mult).then_inc(s_sq, 1)

            @block.scalar
            def _(scalar, c=c):
                g = T0 + 64 * c
                for tt in range(NTT):
                    for kt in range(KD):
                        scalar.wait_ge(s_tr, g + 1)
                        scalar.copy(out=xn_fm[:, kt * LS + tt * 128: kt * LS + (tt + 1) * 128],
                                    in_=tr_bank(g)[:, 0:128]).then_inc(s_trc, 1)
                        g += 1
                scalar.wait_ge(s_u1, c + 1)
                scalar.activation(out=mneg_bc[:], in_=pt_xi[:], func=AF.Copy,
                                  scale=-1.0 / D).then_inc(s_st, 1)
                scalar.wait_ge(s_st3, c + 1)
                scalar.activation(out=x2s[:], in_=x2s[:], func=AF.Ln,
                                  bias=eps_c[:])
                scalar.activation(out=r_bc[:], in_=x2s[:], func=AF.Exp,
                                  scale=-0.5).then_inc(s_st, 1)

            @block.vector
            def _(vector, c=c):
                vector.wait_ge(s_st, 2 * c + 1)
                vector.wait_ge(s_u2, c + 1)
                vector.tensor_tensor(out=x2s[:], in0=mneg_bc[:], in1=mneg_bc[:],
                                     op=ALU.mult)
                vector.scalar_tensor_tensor(out=x2s[:], in0=pt_dt[:],
                                            scalar=1.0 / D, in1=x2s[:],
                                            op0=ALU.mult,
                                            op1=ALU.subtract).then_inc(s_st3, 1)
                vector.wait_ge(s_st, 2 * c + 2)
                for kt in range(KD):
                    vector.tensor_tensor(out=xn_fm[:, kt * LS:(kt + 1) * LS],
                                         in0=xn_fm[:, kt * LS:(kt + 1) * LS],
                                         in1=mneg_bc[:], op=ALU.add)
                    vector.tensor_tensor(out=xn_fm[:, kt * LS:(kt + 1) * LS],
                                         in0=xn_fm[:, kt * LS:(kt + 1) * LS],
                                         in1=r_bc[:], op=ALU.mult).then_inc(s_lnx, 1)

            @block.tensor
            def _(tensor, c=c):
                tensor.wait_ge(s_lnx, 8 * (c + 1))
                tensor.wait_ge(s_gate, 8 * c)
                for dblk in range(NDB):
                    u_xp = 48 * c + 2 * dblk
                    tensor.wait_ge(s_w, W0 + 16 * (u_xp + 2))
                    tensor.wait_ge(s_evxi, 16 * c + dblk)
                    tensor.wait_ge(s_evdt, 16 * c + dblk)
                    slot_xp = wpool[:, (u_xp % 2) * 1024:(u_xp % 2 + 1) * 1024]
                    slot_dt = wpool[:, ((u_xp + 1) % 2) * 1024:((u_xp + 1) % 2 + 1) * 1024]
                    for kt in range(KD):
                        tensor.matmul(pt_xi[:], lhsT=slot_xp[:, kt * 128:(kt + 1) * 128],
                                      rhs=xn_fm[:, kt * LS:(kt + 1) * LS],
                                      start=(kt == 0), stop=(kt == KD - 1))
                    pe_fence(tensor, pt_xi, s_wfree)
                    for kt in range(KD):
                        tensor.matmul(pt_dt[:], lhsT=slot_dt[:, kt * 128:(kt + 1) * 128],
                                      rhs=xn_fm[:, kt * LS:(kt + 1) * LS],
                                      start=(kt == 0), stop=(kt == KD - 1))
                    pe_fence(tensor, pt_dt, s_wfree)

            @block.scalar
            def _(scalar, c=c):
                for dblk in range(NDB):
                    scalar.wait_ge(s_wfree, 48 * c + 2 * dblk + 1)
                    scalar.wait_ge(s_conv, max(0, 16 * c + dblk - 1))
                    scalar.copy(out=xi_buf[dblk % 2][:, 3:3 + LS],
                                in_=pt_xi[:]).then_inc(s_evxi, 1)
                    scalar.wait_ge(s_wfree, 48 * c + 2 * dblk + 2)
                    if c > 0 and dblk == 0:
                        scalar.wait_ge(s_scan, 256 * c)
                    scalar.activation(out=h_buf[0][:], in_=pt_dt[:], func=AF.Exp,
                                      bias=dtb_c[:, dblk:dblk + 1])
                    scalar.activation(out=dt_sup[:, dblk * LS:(dblk + 1) * LS],
                                      in_=h_buf[0][:], func=AF.Ln,
                                      bias=1.0).then_inc(s_evdt, 1)

            @block.vector
            def _(vector, c=c):
                if c > 0:
                    vector.wait_ge(s_wfree, 48 * c)
                for dblk in range(NDB):
                    vector.wait_ge(s_evxi, 16 * c + dblk + 1)
                    vector.tensor_copy(out=xi_buf[dblk % 2][:, 0:3],
                                       in_=tails[:, dblk * 3:(dblk + 1) * 3])
                    yd = y_sup[:, dblk * LS:(dblk + 1) * LS]
                    vector.tensor_scalar(out=yd, in0=xi_buf[dblk % 2][:, 0:LS],
                                         scalar1=cvw_c[:, dblk * 4:dblk * 4 + 1],
                                         scalar2=cvb_c[:, dblk:dblk + 1],
                                         op0=ALU.mult, op1=ALU.add)
                    for k in range(1, 4):
                        vector.scalar_tensor_tensor(out=yd, in0=xi_buf[dblk % 2][:, k:k + LS],
                                                    scalar=cvw_c[:, dblk * 4 + k:dblk * 4 + k + 1],
                                                    in1=yd, op0=ALU.mult, op1=ALU.add)
                    vector.tensor_copy(out=tails[:, dblk * 3:(dblk + 1) * 3],
                                       in_=xi_buf[dblk % 2][:, LS:LS + 3]).then_inc(s_conv, 1)

            @block.tensor
            def _(tensor, c=c):
                if c > 0:
                    tensor.wait_ge(s_evsm, 2 * c)
                    tensor.wait_ge(s_evbc, 16 * c)
                for kt in range(KD):
                    tensor.matmul(pt_bcb[0:16, :], lhsT=wbc_sb[:, kt * 16:(kt + 1) * 16],
                                  rhs=xn_fm[:, kt * LS:(kt + 1) * LS],
                                  start=(kt == 0), stop=(kt == KD - 1))
                pe_fence(tensor, pt_bcb, s_bcmp)
                for kt in range(KD):
                    tensor.matmul(pt_bcc[0:16, :], lhsT=wbc_sb[:, (KD + kt) * 16:(KD + kt + 1) * 16],
                                  rhs=xn_fm[:, kt * LS:(kt + 1) * LS],
                                  start=(kt == 0), stop=(kt == KD - 1))
                pe_fence(tensor, pt_bcc, s_bcmp)
                tensor.wait_ge(s_evxi, 16 * (c + 1))
                for kt in range(KD):
                    tensor.matmul(pt_xi[0:64, :], lhsT=ffn_win[:, kt * 64:(kt + 1) * 64],
                                  rhs=xn_fm[:, kt * LS:(kt + 1) * LS],
                                  start=(kt == 0), stop=(kt == KD - 1))
                pe_fence(tensor, pt_xi, s_ffnmm)

            @block.scalar
            def _(scalar, c=c):
                for dblk in range(NDB):
                    scalar.wait_ge(s_conv, 16 * c + dblk + 1)
                    scalar.wait_ge(s_dtx, max(0, 16 * c + dblk - 1))
                    scalar.activation(out=xc_buf[dblk % 2][:],
                                      in_=y_sup[:, dblk * LS:(dblk + 1) * LS],
                                      func=AF.Silu).then_inc(s_xc, 1)

            @block.vector
            def _(vector, c=c):
                for dblk in range(NDB):
                    vector.wait_ge(s_xc, 16 * c + dblk + 1)
                    vector.tensor_tensor(out=dtx_sup[:, dblk * LS:(dblk + 1) * LS],
                                         in0=dt_sup[:, dblk * LS:(dblk + 1) * LS],
                                         in1=xc_buf[dblk % 2][:], op=ALU.mult)
                    vector.tensor_scalar(out=y_sup[:, dblk * LS:(dblk + 1) * LS],
                                         in0=xc_buf[dblk % 2][:],
                                         scalar1=d_c[:, dblk:dblk + 1], scalar2=None,
                                         op0=ALU.mult).then_inc(s_dtx, 1)

            @block.scalar
            def _(scalar, c=c):
                for i in range(16):
                    scalar.wait_ge(s_ffnmm, c + 1)
                    scalar.wait_ge(s_gch, max(0, 16 * c + i - 1))
                    scalar.activation(out=h_i[i % 2][:], in_=pt_xi[0:64, :], func=AF.Relu,
                                      scale=ffn_w1[:, i:i + 1],
                                      bias=ffn_b2c[:, i:i + 1]).then_inc(s_hi, 1)

            @block.vector
            def _(vector, c=c):
                for i in range(16):
                    vector.wait_ge(s_hi, 16 * c + i + 1)
                    if i == 0:
                        vector.tensor_scalar(out=g_fm[0][:], in0=h_i[0][:],
                                             scalar1=ffn_w2[:, 0:1], scalar2=ffn_b2s[:],
                                             op0=ALU.mult, op1=ALU.add).then_inc(s_gch, 1)
                    else:
                        vector.scalar_tensor_tensor(out=g_fm[i % 2][:], in0=h_i[i % 2][:],
                                                    scalar=ffn_w2[:, i:i + 1],
                                                    in1=g_fm[(i - 1) % 2][:],
                                                    op0=ALU.mult, op1=ALU.add).then_inc(s_gch, 1)

            @block.tensor
            def _(tensor, c=c):
                tensor.wait_ge(s_evsm, 2 * (c + 1))
                for s in range(S):
                    tensor.wait_ge(s_evbc, 16 * c + s)
                    tensor.matmul(pt_bcb[:], lhsT=es_all[:, s * 128:(s + 1) * 128],
                                  rhs=bc_cmp[:, 0:LS], start=True, stop=True)
                    pe_fence(tensor, pt_bcb, s_bcmm)
                    tensor.matmul(pt_bcc[:], lhsT=es_all[:, s * 128:(s + 1) * 128],
                                  rhs=bc_cmp[:, LS:2 * LS], start=True, stop=True)
                    pe_fence(tensor, pt_bcc, s_bcmm)

            @block.scalar
            def _(scalar, c=c):
                scalar.wait_ge(s_bcmp, 2 * c + 2)
                scalar.wait_ge(s_scan, 256 * c)
                scalar.copy(out=bc_cmp[:, 0:LS], in_=pt_bcb[0:16, :])
                scalar.copy(out=bc_cmp[:, LS:2 * LS], in_=pt_bcc[0:16, :]).then_inc(s_evsm, 2)
                for s in range(S):
                    scalar.wait_ge(s_bcmm, 32 * c + 2 * s + 2)
                    scalar.wait_ge(s_scan, max(0, 256 * c + 16 * (s - 1)))
                    scalar.copy(out=b_bc[s % 2][:], in_=pt_bcb[:])
                    scalar.copy(out=c_bc[s % 2][:], in_=pt_bcc[:]).then_inc(s_evbc, 1)
                    for dblk in range(NDB):
                        g = 256 * c + s * 16 + dblk
                        gl = s * 16 + dblk
                        scalar.wait_ge(s_scan, max(0, g - 1))
                        if gl >= 2:
                            scalar.copy(out=state[:, gl - 2:gl - 1],
                                        in_=h_buf[gl % 2][:, LS - 1:LS])
                        scalar.activation(out=a_buf[g % 2][:],
                                          in_=dt_sup[:, dblk * LS:(dblk + 1) * LS],
                                          func=AF.Exp,
                                          scale=a_cols[:, dblk * S + s:dblk * S + s + 1]
                                          ).then_inc(s_a, 1)
                scalar.wait_ge(s_scan, 256 * (c + 1))
                scalar.copy(out=state[:, 254:255], in_=h_buf[0][:, LS - 1:LS])
                scalar.copy(out=state[:, 255:256], in_=h_buf[1][:, LS - 1:LS])

            @block.vector
            def _(vector, c=c):
                for s in range(S):
                    vector.wait_ge(s_evbc, 16 * c + s + 1)
                    for dblk in range(NDB):
                        g = 256 * c + s * 16 + dblk
                        gl = s * 16 + dblk
                        vector.wait_ge(s_a, g + 1)
                        vector.tensor_tensor(out=uz_buf[:],
                                             in0=dtx_sup[:, dblk * LS:(dblk + 1) * LS],
                                             in1=b_bc[s % 2][:], op=ALU.mult)
                        vector.tensor_tensor_scan(out=h_buf[gl % 2][:], data0=a_buf[g % 2][:],
                                                  data1=uz_buf[:],
                                                  initial=state[:, gl:gl + 1],
                                                  op0=ALU.mult, op1=ALU.add)
                        vector.tensor_tensor(out=uz_buf[:], in0=h_buf[gl % 2][:],
                                             in1=c_bc[s % 2][:], op=ALU.mult)
                        vector.tensor_tensor(out=y_sup[:, dblk * LS:(dblk + 1) * LS],
                                             in0=y_sup[:, dblk * LS:(dblk + 1) * LS],
                                             in1=uz_buf[:], op=ALU.add).then_inc(s_scan, 1)

            @block.tensor
            def _(tensor, c=c):
                def tr_prod(ob):
                    j = 8 * c + ob
                    tensor.wait_ge(s_prod, j + 1)
                    for q in range(NTT):
                        g = T0 + 64 * c + 32 + 4 * ob + q
                        tensor.wait_ge(s_trc, g - 1)
                        tensor.transpose(tr_bank(g)[:, 0:128],
                                         in_=prod[:, q * 128:(q + 1) * 128],
                                         identity=ident[:])
                        pe_fence(tensor, tr_bank(g), s_tr)

                tensor.wait_ge(s_scan, 256 * (c + 1))
                tensor.wait_ge(s_gch, 16 * (c + 1))
                tensor.wait_ge(s_hi, 16 * (c + 1))
                for ob in range(NOB):
                    j = 8 * c + ob
                    u_a = 48 * c + 32 + 2 * ob
                    tensor.wait_ge(s_prod, max(0, j - 1))
                    tensor.wait_ge(s_w, W0 + 16 * (u_a + 1))
                    slot_a = wpool[:, (u_a % 2) * 1024:(u_a % 2 + 1) * 1024]
                    slot_b = wpool[:, ((u_a + 1) % 2) * 1024:((u_a + 1) % 2 + 1) * 1024]
                    for dblk in range(8):
                        tensor.matmul(pt_out[ob % 2][:],
                                      lhsT=slot_a[:, dblk * 128:(dblk + 1) * 128],
                                      rhs=y_sup[:, dblk * LS:(dblk + 1) * LS],
                                      start=(dblk == 0), stop=False)
                    tensor.wait_ge(s_w, W0 + 16 * (u_a + 2))
                    for dblk in range(8, NDB):
                        tensor.matmul(pt_out[ob % 2][:],
                                      lhsT=slot_b[:, (dblk - 8) * 128:(dblk - 7) * 128],
                                      rhs=y_sup[:, dblk * LS:(dblk + 1) * LS],
                                      start=False, stop=(dblk == NDB - 1))
                    pe_fence(tensor, pt_out[ob % 2], s_wfree, k=2)
                    tensor.wait_ge(s_gate, j)
                    tensor.matmul(pt_xi[:], lhsT=ffn_wo[:, ob * 128:(ob + 1) * 128],
                                  rhs=g_fm[1][:], start=True, stop=True)
                    pe_fence(tensor, pt_xi, s_fpre)
                    if ob >= 1:
                        tr_prod(ob - 1)
                tr_prod(NOB - 1)

            @block.scalar
            def _(scalar, c=c):
                for ob in range(NOB):
                    j = 8 * c + ob
                    scalar.wait_ge(s_fpre, j + 1)
                    scalar.wait_ge(s_prod, max(0, j - 1))
                    scalar.activation(out=g_tile[ob % 2][:], in_=pt_xi[:], func=AF.Silu,
                                      bias=ffn_boc[:, ob:ob + 1]).then_inc(s_gate, 1)

            @block.vector
            def _(vector, c=c):
                for ob in range(NOB):
                    j = 8 * c + ob
                    vector.wait_ge(s_wfree, 48 * c + 32 + 2 * ob + 2)
                    vector.wait_ge(s_gate, j + 1)
                    if j >= 1:
                        vector.wait_ge(s_tr, tr_prod_end(j - 1))
                    vector.tensor_tensor(out=prod[:], in0=pt_out[ob % 2][:],
                                         in1=g_tile[ob % 2][:],
                                         op=ALU.mult).then_inc(s_prod, 1)
                    for q in range(NTT):
                        g = T0 + 64 * c + 32 + 4 * ob + q
                        jj = 32 * c + 4 * ob + q
                        vector.wait_ge(s_tr, g + 1)
                        if jj >= 2:
                            vector.wait_ge(s_ost, 16 * (jj - 1))
                        vector.tensor_tensor(
                            out=out_stg[jj % 2][:],
                            in0=tr_bank(g)[:, 0:128],
                            in1=x_tok[:, q * 1024 + ob * 128: q * 1024 + (ob + 1) * 128],
                            op=ALU.add).then_inc(s_trc, 1)

            @block.gpsimd
            def _(gpsimd, c=c, t0=t0):
                for ob in range(NOB):
                    for q in range(NTT):
                        j = 32 * c + 4 * ob + q
                        gpsimd.wait_ge(s_trc, T0 + 64 * c + 32 + 4 * ob + q + 1)
                        gpsimd.dma_start(out=out_hbm[t0 + q * 128: t0 + (q + 1) * 128,
                                                     ob * 128:(ob + 1) * 128],
                                         in_=out_stg[j % 2][:]).then_inc(s_ost, 16)

            if c + 1 < nsc:
                @block.sync
                def _(sync, c=c):
                    sync.wait_ge(s_trc, trc_all_end(c))
                    nt0 = (c + 1) * LS
                    for tt in range(NTT):
                        sync.dma_start(out=x_tok[:, tt * 1024:(tt + 1) * 1024],
                                       in_=x_hbm[nt0 + tt * 128: nt0 + (tt + 1) * 128, :]
                                       ).then_inc(s_x, 16)

        NOST = 16 * 32 * nsc

        @block.gpsimd
        def _(gpsimd):
            gpsimd.wait_ge(s_ost, NOST)

        if taps:
            @block.sync
            def _(sync):
                sync.wait_ge(s_ost, NOST)
                n = NOST
                for name, ap in taps.items():
                    srcap = {"xn_fm": xn_fm, "dt": dt_sup, "dtx": dtx_sup,
                             "y": y_sup, "bbc": b_bc[1], "state": state,
                             "ident": ident, "es": es_all, "xtok": x_tok,
                             "mrow": mneg_bc, "m2row": r_bc,
                             }[name]
                    sync.dma_start(out=ap, in_=srcap[:]).then_inc(s_ost, 16)
                    n += 16
                sync.wait_ge(s_ost, n)

    return nc


_NC_CACHE = {}


def kernel(L0_attn_x, L0_attn_x_proj_w, L0_attn_dt_proj_w, L0_attn_dt_proj_b,
           L0_attn_A_log, L0_attn_D, L0_attn_B_proj_w, L0_attn_C_proj_w,
           L0_attn_conv_w, L0_attn_conv_b, L0_attn_out_proj_w,
           L0_ffn_Win, L0_ffn_bin_, L0_ffn_W1, L0_ffn_b1, L0_ffn_W2,
           L0_ffn_b2, L0_ffn_Wout, L0_ffn_bout):
    x = np.ascontiguousarray(np.asarray(L0_attn_x, dtype=np.float32))
    w = {
        "x_proj_w": L0_attn_x_proj_w, "dt_proj_w": L0_attn_dt_proj_w,
        "dt_proj_b": L0_attn_dt_proj_b, "A_log": L0_attn_A_log,
        "D_vec": L0_attn_D, "B_proj_w": L0_attn_B_proj_w,
        "C_proj_w": L0_attn_C_proj_w,
        "conv_w": np.asarray(L0_attn_conv_w).reshape(DI, 4),
        "conv_b": L0_attn_conv_b, "out_proj_w": L0_attn_out_proj_w,
        "ffn_Win": L0_ffn_Win, "ffn_bin": L0_ffn_bin_, "ffn_W1": L0_ffn_W1,
        "ffn_b1": L0_ffn_b1, "ffn_W2": L0_ffn_W2, "ffn_b2": L0_ffn_b2,
        "ffn_Wout": L0_ffn_Wout, "ffn_bout": L0_ffn_bout,
    }
    w = {k: np.ascontiguousarray(np.asarray(v, dtype=np.float32)) for k, v in w.items()}
    if "nc" not in _NC_CACHE:
        _NC_CACHE["nc"] = build_nc()
    nc = _NC_CACHE["nc"]
    in_maps = [dict(w, x=np.ascontiguousarray(x[b])) for b in range(8)]
    res = run_bass_kernel_spmd(nc, in_maps, core_ids=list(range(8)))
    return np.stack([r["out"] for r in res.results], axis=0)


# version 07
# speedup vs baseline: 1.0456x; 1.0456x over previous
import os
import numpy as np
from contextlib import ExitStack

import concourse.bass as bass
import concourse.mybir as mybir
from concourse.bass_utils import run_bass_kernel_spmd

F32 = mybir.dt.float32
I32 = mybir.dt.int32
AF = mybir.ActivationFunctionType
ALU = mybir.AluOpType
AX = mybir.AxisListType

T = 4096
D = 1024
DI = 2048
S = 16
LS = 512
NSC = T // LS
NDB = DI // 128
KD = D // 128
NOB = D // 128
NTT = LS // 128
EPS = 1e-5

W0 = 16 * 40
T0 = 384 + 16


def build_nc(debug_taps=(), nsc=NSC):
    nc = bass.Bass("TRN2", target_bir_lowering=False, detect_race_conditions=False)

    x_hbm = nc.dram_tensor("x", [T, D], F32, kind="ExternalInput").ap()
    w_xp = nc.dram_tensor("x_proj_w", [DI, D], F32, kind="ExternalInput").ap()
    w_dt = nc.dram_tensor("dt_proj_w", [DI, D], F32, kind="ExternalInput").ap()
    b_dt = nc.dram_tensor("dt_proj_b", [DI], F32, kind="ExternalInput").ap()
    a_log = nc.dram_tensor("A_log", [DI, S], F32, kind="ExternalInput").ap()
    d_vec = nc.dram_tensor("D_vec", [DI], F32, kind="ExternalInput").ap()
    w_b = nc.dram_tensor("B_proj_w", [S, D], F32, kind="ExternalInput").ap()
    w_c = nc.dram_tensor("C_proj_w", [S, D], F32, kind="ExternalInput").ap()
    w_cv = nc.dram_tensor("conv_w", [DI, 4], F32, kind="ExternalInput").ap()
    b_cv = nc.dram_tensor("conv_b", [DI], F32, kind="ExternalInput").ap()
    w_op = nc.dram_tensor("out_proj_w", [D, DI], F32, kind="ExternalInput").ap()
    f_win = nc.dram_tensor("ffn_Win", [D, 64], F32, kind="ExternalInput").ap()
    f_bin = nc.dram_tensor("ffn_bin", [64], F32, kind="ExternalInput").ap()
    f_w1 = nc.dram_tensor("ffn_W1", [64, 16], F32, kind="ExternalInput").ap()
    f_b1 = nc.dram_tensor("ffn_b1", [64, 16], F32, kind="ExternalInput").ap()
    f_w2 = nc.dram_tensor("ffn_W2", [64, 16], F32, kind="ExternalInput").ap()
    f_b2 = nc.dram_tensor("ffn_b2", [64], F32, kind="ExternalInput").ap()
    f_wo = nc.dram_tensor("ffn_Wout", [64, D], F32, kind="ExternalInput").ap()
    f_bo = nc.dram_tensor("ffn_bout", [D], F32, kind="ExternalInput").ap()
    out_hbm = nc.dram_tensor("out", [T, D], F32, kind="ExternalOutput").ap()

    scr_xp = nc.dram_tensor("scr_xp", [NDB, 128, KD * 128], F32, kind="Internal").ap()
    scr_dt = nc.dram_tensor("scr_dt", [NDB, 128, KD * 128], F32, kind="Internal").ap()
    scr_op = nc.dram_tensor("scr_op", [NOB, 128, NDB * 128], F32, kind="Internal").ap()

    sc0taps = "sc0" in debug_taps
    taps = {}
    for name, shape in [("xn_fm", [128, KD * LS]), ("dt", [128, NDB * LS]),
                        ("dtx", [128, NDB * LS]), ("y", [128, NDB * LS]),
                        ("bbc", [128, LS]), ("state", [128, 256]),
                        ("ident", [128, 128]), ("es", [16, 16 * 128]),
                        ("xtok", [128, NTT * 1024]), ("mrow", [128, LS]),
                        ("m2row", [128, LS])]:
        if name in debug_taps:
            taps[name] = nc.dram_tensor("dbg_" + name, shape, F32, kind="ExternalOutput").ap()
    s0taps = {}
    if sc0taps:
        for name, shape in [("s0xn", [128, KD * LS]), ("s0b", [128, LS]),
                            ("s0c", [128, LS]), ("s0dt", [128, NDB * LS]),
                            ("s0dtx", [128, NDB * LS]), ("s0y", [128, NDB * LS])]:
            s0taps[name] = nc.dram_tensor("dbg_" + name, shape, F32, kind="ExternalOutput").ap()

    with ExitStack() as stack:
        def sb(name, shape, dt=F32):
            return stack.enter_context(nc.sbuf_tensor(name, shape, dt))

        def ps(name, shape, dt=F32):
            return stack.enter_context(nc.psum_tensor(name, shape, dt))

        x_tok = sb("x_tok", [128, NTT * 1024])
        out_stg = [sb(f"out_stg{i}", [128, 128]) for i in range(2)]
        x2s = sb("x2s", [128, LS])
        mneg_bc = sb("mneg_bc", [128, LS])
        r_bc = sb("r_bc", [128, LS])
        ones_f = sb("ones_f", [128, 128])
        xn_fm = sb("xn_fm", [128, KD * LS])
        dt_sup = sb("dt_sup", [128, NDB * LS])
        dtx_sup = sb("dtx_sup", [128, NDB * LS])
        y_sup = sb("y_sup", [128, NDB * LS])
        wpool = sb("wpool", [128, 2 * 1024])
        g_fm = [sb(f"g_fm{i}", [64, LS]) for i in range(2)]
        bc_cmp = sb("bc_cmp", [16, 2 * LS])
        b_bc = [sb(f"b_bc{i}", [128, LS]) for i in range(2)]
        c_bc = [sb(f"c_bc{i}", [128, LS]) for i in range(2)]
        a_buf = [sb(f"a_buf{i}", [128, LS]) for i in range(2)]
        h_buf = [sb(f"h_buf{i}", [128, LS]) for i in range(2)]
        uz_buf = sb("uz_buf", [128, LS])
        xc_buf = [sb(f"xc_buf{i}", [128, LS]) for i in range(2)]
        xi_buf = [sb(f"xi_buf{i}", [128, LS + 3]) for i in range(2)]
        prod = sb("prod", [128, LS])
        g_tile = [sb(f"g_tile{i}", [128, LS]) for i in range(2)]
        h_i = [sb(f"h_i{i}", [64, LS]) for i in range(2)]
        state = sb("state", [128, 256])
        a_cols = sb("a_cols", [128, NDB * S])
        dtb_c = sb("dtb_c", [128, NDB])
        d_c = sb("d_c", [128, NDB])
        cvb_c = sb("cvb_c", [128, NDB])
        cvw_c = sb("cvw_c", [128, NDB * 4])
        tails = sb("tails", [128, NDB * 3])
        ident = sb("ident", [128, 128])
        es_all = sb("es_all", [16, S * 128])
        wbc_sb = sb("wbc_sb", [128, 2 * KD * 16])
        ffn_win = sb("ffn_win", [128, KD * 64])
        ffn_wo = sb("ffn_wo", [64, D])
        ffn_w1 = sb("ffn_w1", [64, 16])
        ffn_b2c = sb("ffn_b2c", [64, 16])
        ffn_w2 = sb("ffn_w2", [64, 16])
        ffn_b2s = sb("ffn_b2s", [64, 1])
        ffn_binc = sb("ffn_binc", [64, 1])
        ffn_boc = sb("ffn_boc", [128, NOB])
        eps_c = sb("eps_c", [128, 1])
        zcol = sb("zcol", [128, 1])

        pt_tr = ps("pt_tr", [128, 512])
        pt_xi = ps("pt_xi", [128, 512])
        pt_dt = ps("pt_dt", [128, 512])
        pt_bcb = ps("pt_bcb", [128, 512])
        pt_bcc = ps("pt_bcc", [128, 512])
        pt_tr2 = ps("pt_tr2", [128, 512])
        pt_out = [ps(f"pt_out{i}", [128, 512]) for i in range(2)]

        def tr_bank(g):
            return pt_tr if g % 2 == 0 else pt_tr2

        def ln_bank(g):
            return (pt_tr, pt_tr2, pt_bcb, pt_bcc)[g % 4]

        def pe_fence(tensor, bank, sem, k=1):
            tensor.matmul(bank[0:1, 0:1], lhsT=zcol[:], rhs=zcol[:],
                          start=False, stop=True,
                          skip_group_check=True).then_inc(sem, k)

        def sm(name):
            return stack.enter_context(nc.semaphore(name))

        s_x = sm("s_x"); s_w = sm("s_w"); s_wfree = sm("s_wfree")
        s_scr = sm("s_scr"); s_ld0 = sm("s_ld0"); s_const = sm("s_const")
        s_gpc = sm("s_gpc")
        s_lnv = sm("s_lnv"); s_lnm = sm("s_lnm"); s_lnz = sm("s_lnz")
        s_lns = sm("s_lns"); s_lnx = sm("s_lnx")
        s_tr = sm("s_tr"); s_trc = sm("s_trc")
        s_evxi = sm("s_evxi"); s_evdt = sm("s_evdt"); s_conv = sm("s_conv")
        s_bcmp = sm("s_bcmp"); s_evsm = sm("s_evsm"); s_ffnmm = sm("s_ffnmm")
        s_hi = sm("s_hi"); s_gch = sm("s_gch"); s_xc = sm("s_xc"); s_dtx = sm("s_dtx")
        s_bcmm = sm("s_bcmm"); s_evbc = sm("s_evbc")
        s_a = sm("s_a"); s_scan = sm("s_scan")
        s_fpre = sm("s_fpre"); s_gate = sm("s_gate"); s_prod = sm("s_prod")
        s_flt = sm("s_flt")
        s_sq = sm("s_sq"); s_u1 = sm("s_u1"); s_u2 = sm("s_u2")
        s_st = sm("s_st"); s_st3 = sm("s_st3"); s_lnb = sm("s_lnb")
        s_u2r = sm("s_u2r"); s_dbg = sm("s_dbg")
        s_ost = sm("s_ost")

        block = stack.enter_context(nc.Block())

        @block.sync
        def _(sync):
            loads = [
                (a_cols[:].rearrange("p (a s) -> p a s", s=16),
                 a_log.rearrange("(a p) s -> p a s", p=128)),
                (dtb_c[:], b_dt.rearrange("(a p) -> p a", p=128)),
                (d_c[:], d_vec.rearrange("(a p) -> p a", p=128)),
                (cvb_c[:], b_cv.rearrange("(a p) -> p a", p=128)),
                (cvw_c[:].rearrange("p (a k) -> p a k", k=4),
                 w_cv.rearrange("(a p) k -> p a k", p=128)),
                (ffn_win[:].rearrange("p (a m) -> p a m", m=64),
                 f_win.rearrange("(a p) m -> p a m", p=128)),
                (ffn_wo[:], f_wo),
                (ffn_w1[:], f_w1),
                (ffn_b2c[:], f_b1),
                (ffn_w2[:], f_w2),
                (ffn_b2s[:], f_b2.rearrange("(m o) -> m o", o=1)),
                (ffn_binc[:], f_bin.rearrange("(m o) -> m o", o=1)),
                (ffn_boc[:], f_bo.rearrange("(a p) -> p a", p=128)),
                (dt_sup[0:16, 4096:5120], w_b),
                (dt_sup[0:16, 5120:6144], w_c),
            ]
            for dst, src in loads:
                sync.dma_start(out=dst, in_=src,
                               allow_slow_non_contiguous=True).then_inc(s_ld0, 16)
        NLD0 = 15 * 16

        @block.gpsimd
        def _(gpsimd):
            ii = dt_sup[:, 0:128].bitcast(I32)
            ic = dt_sup[:, 256:257].bitcast(I32)
            gpsimd.iota(ii, pattern=[[1, 128]], base=0, channel_multiplier=0)
            gpsimd.iota(ic, pattern=[[1, 1]], base=0, channel_multiplier=1)
            gpsimd.tensor_copy(out=ident[:], in_=ii)
            gpsimd.tensor_copy(out=dtx_sup[:, 0:1], in_=ic)
            gpsimd.tensor_scalar(out=ident[:], in0=ident[:], scalar1=dtx_sup[:, 0:1],
                                 scalar2=None, op0=ALU.is_equal)
            gpsimd.memset(eps_c[:], EPS)
            gpsimd.memset(zcol[:], 0.0)
            gpsimd.memset(ones_f[:], 1.0)
            gpsimd.memset(state[:], 0.0)
            gpsimd.memset(tails[:], 0.0).then_inc(s_gpc, 1)

        @block.scalar
        def _(scalar):
            scalar.wait_ge(s_ld0, NLD0)
            scalar.activation(out=a_cols[:], in_=a_cols[:], func=AF.Exp)
            scalar.activation(out=a_cols[:], in_=a_cols[:],
                              func=AF.Copy, scale=-1.0).then_inc(s_const, 1)

        @block.vector
        def _(vector):
            vector.wait_ge(s_ld0, NLD0)
            vector.wait_ge(s_gpc, 1)
            vector.tensor_copy(out=es_all[:].rearrange("p (s m) -> p s m", m=128),
                               in_=ident[0:16, 0:16].unsqueeze(2).broadcast_to([16, 16, 128]))
            vector.scalar_tensor_tensor(out=ffn_b2c[:], in0=ffn_w1[:],
                                        scalar=ffn_binc[:], in1=ffn_b2c[:],
                                        op0=ALU.mult, op1=ALU.add).then_inc(s_const, 1)

        pre_plan = []
        for blk in range(NDB):
            pre_plan.append((w_xp[blk * 128:(blk + 1) * 128, :], scr_xp[blk], KD))
        for blk in range(NDB):
            pre_plan.append((w_dt[blk * 128:(blk + 1) * 128, :], scr_dt[blk], KD))
        for blk in range(NOB):
            pre_plan.append((w_op[blk * 128:(blk + 1) * 128, :], scr_op[blk], NDB))
        NPRE = sum(p[2] for p in pre_plan)

        def pre_wslot(i, nkt):
            if nkt <= 8:
                return wpool[:, (i % 2) * 1024: (i % 2) * 1024 + nkt * 128]
            return dt_sup[:, 6144: 6144 + nkt * 128]

        @block.sync
        def _(sync):
            for i, (srcap, dst, nkt) in enumerate(pre_plan):
                if nkt <= 8:
                    sync.wait_ge(s_scr, max(0, i - 1) * 16)
                else:
                    sync.wait_ge(s_scr, i * 16)
                sync.dma_start(out=pre_wslot(i, nkt), in_=srcap).then_inc(s_w, 16)
                stage = xn_fm[:, (i % 2) * 2048: (i % 2) * 2048 + nkt * 128]
                trc_after = sum(p[2] for p in pre_plan[:i + 1])
                sync.wait_ge(s_trc, trc_after)
                sync.dma_start(out=dst, in_=stage).then_inc(s_scr, 16)

        @block.tensor
        def _(tensor):
            tensor.wait_ge(s_const, 2)
            g = 0
            for i, (srcap, dst, nkt) in enumerate(pre_plan):
                wslot = pre_wslot(i, nkt)
                tensor.wait_ge(s_w, (i + 1) * 16)
                for kt in range(nkt):
                    tensor.wait_ge(s_trc, max(0, g - 1))
                    tensor.transpose(tr_bank(g)[:, 0:128],
                                     in_=wslot[:, kt * 128:(kt + 1) * 128],
                                     identity=ident[:])
                    pe_fence(tensor, tr_bank(g), s_tr)
                    g += 1
            tensor.wait_ge(s_ld0, NLD0)
            for j in range(2 * KD):
                g = NPRE + j
                tensor.wait_ge(s_trc, g - 1)
                tensor.transpose(tr_bank(g)[:, 0:16],
                                 in_=dt_sup[0:16, 4096 + j * 128: 4096 + (j + 1) * 128],
                                 identity=ident[0:16, 0:16])
                pe_fence(tensor, tr_bank(g), s_tr)

        @block.scalar
        def _(scalar):
            g = 0
            for i, (srcap, dst, nkt) in enumerate(pre_plan):
                stage = xn_fm[:, (i % 2) * 2048: (i % 2) * 2048 + nkt * 128]
                for kt in range(nkt):
                    scalar.wait_ge(s_tr, g + 1)
                    scalar.copy(out=stage[:, kt * 128:(kt + 1) * 128],
                                in_=tr_bank(g)[:, 0:128]).then_inc(s_trc, 1)
                    g += 1
            for j in range(2 * KD):
                g = NPRE + j
                scalar.wait_ge(s_tr, g + 1)
                scalar.copy(out=wbc_sb[:, j * 16:(j + 1) * 16],
                            in_=tr_bank(g)[:, 0:16]).then_inc(s_trc, 1)

        @block.sync
        def _(sync):
            for tt in range(NTT):
                sync.dma_start(out=x_tok[:, tt * 1024:(tt + 1) * 1024],
                               in_=x_hbm[tt * 128:(tt + 1) * 128, :]).then_inc(s_x, 16)

        def trc_xn_end(c):
            return T0 + 64 * c + 32

        def trc_all_end(c):
            return T0 + 64 * (c + 1)

        def tr_xn_tile_end(gt):
            c_, tt_ = gt // 4, gt % 4
            return T0 + 64 * c_ + 8 * (tt_ + 1)

        def tr_prod_end(j):
            c_, ob_ = j // 8, j % 8
            return T0 + 64 * c_ + 32 + 4 * (ob_ + 1)

        for c in range(nsc):
            t0 = c * LS

            @block.sync
            def _(sync, c=c):
                srcs = []
                for dblk in range(NDB):
                    srcs.append(scr_xp[dblk])
                    srcs.append(scr_dt[dblk])
                for ob in range(NOB):
                    srcs.append(scr_op[ob][:, 0:1024])
                    srcs.append(scr_op[ob][:, 1024:2048])
                for u, sap in enumerate(srcs):
                    gs = 48 * c + u
                    if gs < 2:
                        sync.wait_ge(s_trc, T0)
                    else:
                        sync.wait_ge(s_wfree, gs - 1)
                    slot = wpool[:, (gs % 2) * 1024:(gs % 2 + 1) * 1024]
                    sync.dma_start(out=slot, in_=sap).then_inc(s_w, 16)

            @block.tensor
            def _(tensor, c=c):
                g0 = T0 + 64 * c
                g = g0
                for tt in range(NTT):
                    tensor.wait_ge(s_x, 16 * (4 * c + tt + 1))
                    for kt in range(KD):
                        k = g - g0
                        if k < 2:
                            tensor.wait_ge(s_trc, g - 1)
                        elif k == 2:
                            tensor.wait_ge(s_evbc, 16 * c)
                        else:
                            tensor.wait_ge(s_trc, g - 3)
                        tensor.transpose(ln_bank(g)[:, 0:128],
                                         in_=x_tok[:, tt * 1024 + kt * 128: tt * 1024 + (kt + 1) * 128],
                                         identity=ident[:])
                        pe_fence(tensor, ln_bank(g), s_tr)
                        g += 1
                tensor.wait_ge(s_gate, 8 * c)
                tensor.wait_ge(s_evxi, 16 * c)
                tensor.wait_ge(s_evdt, 16 * c)
                tensor.wait_ge(s_trc, trc_xn_end(c))
                for kt in range(KD):
                    tensor.matmul(pt_xi[:], lhsT=ones_f[:],
                                  rhs=xn_fm[:, kt * LS:(kt + 1) * LS],
                                  start=(kt == 0), stop=(kt == KD - 1))
                pe_fence(tensor, pt_xi, s_u1)
                for kt in range(KD):
                    tensor.wait_ge(s_sq, 8 * c + kt + 1)
                    tensor.matmul(pt_dt[:], lhsT=ones_f[:],
                                  rhs=x2s[:],
                                  start=(kt == 0),
                                  stop=(kt == KD - 1)).then_inc(s_u2r, 1)
                pe_fence(tensor, pt_dt, s_u2)

            @block.scalar
            def _(scalar, c=c):
                g = T0 + 64 * c
                if sc0taps and c == 1:
                    scalar.wait_ge(s_dbg, 16)
                g0 = g
                for tt in range(NTT):
                    for kt in range(KD):
                        if g - g0 == 31:
                            scalar.wait_ge(s_tr, g + 1)
                            scalar.copy(out=mneg_bc[0:1, 0:1],
                                        in_=ln_bank(g)[0:1, 0:1])
                        else:
                            scalar.wait_ge(s_tr, g + 2)
                        scalar.copy(out=xn_fm[:, kt * LS + tt * 128: kt * LS + (tt + 1) * 128],
                                    in_=ln_bank(g)[:, 0:128]).then_inc(s_trc, 1)
                        g += 1
                for kt in range(KD):
                    if kt >= 1:
                        scalar.wait_ge(s_u2r, 8 * c + kt)
                    scalar.activation(out=x2s[:],
                                      in_=xn_fm[:, kt * LS:(kt + 1) * LS],
                                      func=AF.Square).then_inc(s_sq, 1)
                scalar.wait_ge(s_u1, c + 1)
                scalar.activation(out=mneg_bc[:], in_=pt_xi[:], func=AF.Copy,
                                  scale=-1.0 / D).then_inc(s_st, 1)
                scalar.wait_ge(s_st3, c + 1)
                scalar.activation(out=r_bc[:], in_=r_bc[:], func=AF.Ln,
                                  bias=eps_c[:])
                scalar.activation(out=r_bc[:], in_=r_bc[:], func=AF.Exp,
                                  scale=-0.5).then_inc(s_st, 1)

            @block.vector
            def _(vector, c=c):
                vector.wait_ge(s_st, 2 * c + 1)
                for kt in range(KD):
                    vector.tensor_tensor(out=xn_fm[:, kt * LS:(kt + 1) * LS],
                                         in0=xn_fm[:, kt * LS:(kt + 1) * LS],
                                         in1=mneg_bc[:], op=ALU.add)
                vector.tensor_tensor(out=r_bc[:], in0=mneg_bc[:], in1=mneg_bc[:],
                                     op=ALU.mult)
                vector.wait_ge(s_u2, c + 1)
                vector.scalar_tensor_tensor(out=r_bc[:], in0=pt_dt[:],
                                            scalar=1.0 / D, in1=r_bc[:],
                                            op0=ALU.mult,
                                            op1=ALU.subtract).then_inc(s_st3, 1)
                vector.wait_ge(s_st, 2 * c + 2)
                for kt in range(KD):
                    vector.tensor_tensor(out=xn_fm[:, kt * LS:(kt + 1) * LS],
                                         in0=xn_fm[:, kt * LS:(kt + 1) * LS],
                                         in1=r_bc[:], op=ALU.mult).then_inc(s_lnx, 1)

            if sc0taps and c == 0:
                @block.gpsimd
                def _(gpsimd):
                    gpsimd.wait_ge(s_lnx, 8)
                    gpsimd.dma_start(out=s0taps["s0xn"], in_=xn_fm[:]).then_inc(s_dbg, 16)
                    gpsimd.wait_ge(s_evbc, 1)
                    gpsimd.dma_start(out=s0taps["s0b"], in_=b_bc[0][:]).then_inc(s_dbg, 16)
                    gpsimd.dma_start(out=s0taps["s0c"], in_=c_bc[0][:]).then_inc(s_dbg, 16)
                    gpsimd.wait_ge(s_evdt, 16)
                    gpsimd.dma_start(out=s0taps["s0dt"], in_=dt_sup[:]).then_inc(s_dbg, 16)
                    gpsimd.wait_ge(s_dtx, 16)
                    gpsimd.dma_start(out=s0taps["s0dtx"], in_=dtx_sup[:]).then_inc(s_dbg, 16)
                    gpsimd.wait_ge(s_scan, 256)
                    gpsimd.dma_start(out=s0taps["s0y"], in_=y_sup[:]).then_inc(s_dbg, 16)

            @block.tensor
            def _(tensor, c=c):
                tensor.wait_ge(s_lnx, 8 * (c + 1))
                tensor.wait_ge(s_gate, 8 * c)
                for dblk in range(NDB):
                    u_xp = 48 * c + 2 * dblk
                    tensor.wait_ge(s_w, W0 + 16 * (u_xp + 2))
                    tensor.wait_ge(s_evxi, 16 * c + dblk)
                    tensor.wait_ge(s_evdt, 16 * c + dblk)
                    slot_xp = wpool[:, (u_xp % 2) * 1024:(u_xp % 2 + 1) * 1024]
                    slot_dt = wpool[:, ((u_xp + 1) % 2) * 1024:((u_xp + 1) % 2 + 1) * 1024]
                    for kt in range(KD):
                        tensor.matmul(pt_xi[:], lhsT=slot_xp[:, kt * 128:(kt + 1) * 128],
                                      rhs=xn_fm[:, kt * LS:(kt + 1) * LS],
                                      start=(kt == 0), stop=(kt == KD - 1))
                    pe_fence(tensor, pt_xi, s_wfree)
                    for kt in range(KD):
                        tensor.matmul(pt_dt[:], lhsT=slot_dt[:, kt * 128:(kt + 1) * 128],
                                      rhs=xn_fm[:, kt * LS:(kt + 1) * LS],
                                      start=(kt == 0), stop=(kt == KD - 1))
                    pe_fence(tensor, pt_dt, s_wfree)

            @block.scalar
            def _(scalar, c=c):
                if sc0taps and c == 1:
                    scalar.wait_ge(s_dbg, 64)
                for dblk in range(NDB):
                    scalar.wait_ge(s_wfree, 48 * c + 2 * dblk + 1)
                    scalar.wait_ge(s_conv, max(0, 16 * c + dblk - 1))
                    if dblk == 0:
                        scalar.copy(out=mneg_bc[0:1, 0:1], in_=pt_xi[0:1, 0:1])
                    scalar.copy(out=xi_buf[dblk % 2][:, 3:3 + LS],
                                in_=pt_xi[:]).then_inc(s_evxi, 1)
                    scalar.wait_ge(s_wfree, 48 * c + 2 * dblk + 2)
                    if c > 0 and dblk == 0:
                        scalar.wait_ge(s_scan, 256 * c)
                    scalar.activation(out=h_buf[0][:], in_=pt_dt[:], func=AF.Exp,
                                      bias=dtb_c[:, dblk:dblk + 1])
                    scalar.activation(out=dt_sup[:, dblk * LS:(dblk + 1) * LS],
                                      in_=h_buf[0][:], func=AF.Ln,
                                      bias=1.0).then_inc(s_evdt, 1)

            @block.vector
            def _(vector, c=c):
                if c > 0:
                    vector.wait_ge(s_wfree, 48 * c)
                for dblk in range(NDB):
                    vector.wait_ge(s_evxi, 16 * c + dblk + 1)
                    vector.tensor_copy(out=xi_buf[dblk % 2][:, 0:3],
                                       in_=tails[:, dblk * 3:(dblk + 1) * 3])
                    yd = y_sup[:, dblk * LS:(dblk + 1) * LS]
                    vector.tensor_scalar(out=yd, in0=xi_buf[dblk % 2][:, 0:LS],
                                         scalar1=cvw_c[:, dblk * 4:dblk * 4 + 1],
                                         scalar2=cvb_c[:, dblk:dblk + 1],
                                         op0=ALU.mult, op1=ALU.add)
                    for k in range(1, 4):
                        vector.scalar_tensor_tensor(out=yd, in0=xi_buf[dblk % 2][:, k:k + LS],
                                                    scalar=cvw_c[:, dblk * 4 + k:dblk * 4 + k + 1],
                                                    in1=yd, op0=ALU.mult, op1=ALU.add)
                    vector.tensor_copy(out=tails[:, dblk * 3:(dblk + 1) * 3],
                                       in_=xi_buf[dblk % 2][:, LS:LS + 3]).then_inc(s_conv, 1)

            @block.tensor
            def _(tensor, c=c):
                if c > 0:
                    tensor.wait_ge(s_evsm, 2 * c)
                    tensor.wait_ge(s_evbc, 16 * c)
                for kt in range(KD):
                    tensor.matmul(pt_bcb[0:16, :], lhsT=wbc_sb[:, kt * 16:(kt + 1) * 16],
                                  rhs=xn_fm[:, kt * LS:(kt + 1) * LS],
                                  start=(kt == 0), stop=(kt == KD - 1))
                pe_fence(tensor, pt_bcb, s_bcmp)
                for kt in range(KD):
                    tensor.matmul(pt_bcc[0:16, :], lhsT=wbc_sb[:, (KD + kt) * 16:(KD + kt + 1) * 16],
                                  rhs=xn_fm[:, kt * LS:(kt + 1) * LS],
                                  start=(kt == 0), stop=(kt == KD - 1))
                pe_fence(tensor, pt_bcc, s_bcmp)
                tensor.wait_ge(s_evxi, 16 * (c + 1))
                for kt in range(KD):
                    tensor.matmul(pt_xi[0:64, :], lhsT=ffn_win[:, kt * 64:(kt + 1) * 64],
                                  rhs=xn_fm[:, kt * LS:(kt + 1) * LS],
                                  start=(kt == 0), stop=(kt == KD - 1))
                pe_fence(tensor, pt_xi, s_ffnmm)

            @block.scalar
            def _(scalar, c=c):
                for dblk in range(NDB):
                    scalar.wait_ge(s_conv, 16 * c + dblk + 1)
                    scalar.wait_ge(s_dtx, max(0, 16 * c + dblk - 1))
                    scalar.activation(out=xc_buf[dblk % 2][:],
                                      in_=y_sup[:, dblk * LS:(dblk + 1) * LS],
                                      func=AF.Silu).then_inc(s_xc, 1)

            @block.vector
            def _(vector, c=c):
                if sc0taps and c == 1:
                    vector.wait_ge(s_dbg, 80)
                for dblk in range(NDB):
                    vector.wait_ge(s_xc, 16 * c + dblk + 1)
                    vector.tensor_tensor(out=dtx_sup[:, dblk * LS:(dblk + 1) * LS],
                                         in0=dt_sup[:, dblk * LS:(dblk + 1) * LS],
                                         in1=xc_buf[dblk % 2][:], op=ALU.mult)
                    vector.tensor_scalar(out=y_sup[:, dblk * LS:(dblk + 1) * LS],
                                         in0=xc_buf[dblk % 2][:],
                                         scalar1=d_c[:, dblk:dblk + 1], scalar2=None,
                                         op0=ALU.mult).then_inc(s_dtx, 1)

            @block.scalar
            def _(scalar, c=c):
                for i in range(16):
                    scalar.wait_ge(s_ffnmm, c + 1)
                    scalar.wait_ge(s_gch, max(0, 16 * c + i - 1))
                    scalar.activation(out=h_i[i % 2][:], in_=pt_xi[0:64, :], func=AF.Relu,
                                      scale=ffn_w1[:, i:i + 1],
                                      bias=ffn_b2c[:, i:i + 1]).then_inc(s_hi, 1)

            @block.vector
            def _(vector, c=c):
                for i in range(16):
                    vector.wait_ge(s_hi, 16 * c + i + 1)
                    if i == 0:
                        vector.tensor_scalar(out=g_fm[0][:], in0=h_i[0][:],
                                             scalar1=ffn_w2[:, 0:1], scalar2=ffn_b2s[:],
                                             op0=ALU.mult, op1=ALU.add).then_inc(s_gch, 1)
                    else:
                        vector.scalar_tensor_tensor(out=g_fm[i % 2][:], in0=h_i[i % 2][:],
                                                    scalar=ffn_w2[:, i:i + 1],
                                                    in1=g_fm[(i - 1) % 2][:],
                                                    op0=ALU.mult, op1=ALU.add).then_inc(s_gch, 1)

            @block.tensor
            def _(tensor, c=c):
                tensor.wait_ge(s_evsm, 2 * (c + 1))
                for s in range(S):
                    tensor.wait_ge(s_evbc, 16 * c + s)
                    tensor.matmul(pt_bcb[:], lhsT=es_all[:, s * 128:(s + 1) * 128],
                                  rhs=bc_cmp[:, 0:LS], start=True, stop=True)
                    pe_fence(tensor, pt_bcb, s_bcmm)
                    tensor.matmul(pt_bcc[:], lhsT=es_all[:, s * 128:(s + 1) * 128],
                                  rhs=bc_cmp[:, LS:2 * LS], start=True, stop=True)
                    pe_fence(tensor, pt_bcc, s_bcmm)

            @block.scalar
            def _(scalar, c=c):
                scalar.wait_ge(s_bcmp, 2 * c + 2)
                scalar.wait_ge(s_scan, 256 * c)
                scalar.copy(out=bc_cmp[:, 0:LS], in_=pt_bcb[0:16, :])
                scalar.copy(out=bc_cmp[:, LS:2 * LS], in_=pt_bcc[0:16, :]).then_inc(s_evsm, 2)
                for dblk in range(2):
                    g = 256 * c + dblk
                    scalar.wait_ge(s_scan, max(0, g - 1))
                    scalar.activation(out=a_buf[g % 2][:],
                                      in_=dt_sup[:, dblk * LS:(dblk + 1) * LS],
                                      func=AF.Exp,
                                      scale=a_cols[:, dblk * S:dblk * S + 1]
                                      ).then_inc(s_a, 1)
                for s in range(S):
                    if sc0taps and c == 0 and s == 2:
                        scalar.wait_ge(s_dbg, 48)
                    scalar.wait_ge(s_bcmm, 32 * c + 2 * s + 2)
                    scalar.wait_ge(s_scan, max(0, 256 * c + 16 * (s - 1)))
                    if s == 0:
                        scalar.copy(out=mneg_bc[0:1, 0:1], in_=pt_bcb[0:1, 0:1])
                    scalar.copy(out=b_bc[s % 2][:], in_=pt_bcb[:])
                    scalar.copy(out=c_bc[s % 2][:], in_=pt_bcc[:]).then_inc(s_evbc, 1)
                    for dblk in range(2 if s == 0 else 0, NDB):
                        g = 256 * c + s * 16 + dblk
                        gl = s * 16 + dblk
                        scalar.wait_ge(s_scan, max(0, g - 1))
                        if gl >= 2:
                            scalar.copy(out=state[:, gl - 2:gl - 1],
                                        in_=h_buf[gl % 2][:, LS - 1:LS])
                        scalar.activation(out=a_buf[g % 2][:],
                                          in_=dt_sup[:, dblk * LS:(dblk + 1) * LS],
                                          func=AF.Exp,
                                          scale=a_cols[:, dblk * S + s:dblk * S + s + 1]
                                          ).then_inc(s_a, 1)
                scalar.wait_ge(s_scan, 256 * (c + 1))
                scalar.copy(out=state[:, 254:255], in_=h_buf[0][:, LS - 1:LS])
                scalar.copy(out=state[:, 255:256], in_=h_buf[1][:, LS - 1:LS])

            @block.vector
            def _(vector, c=c):
                for s in range(S):
                    vector.wait_ge(s_evbc, 16 * c + s + 1)
                    for dblk in range(NDB):
                        g = 256 * c + s * 16 + dblk
                        gl = s * 16 + dblk
                        vector.wait_ge(s_a, g + 1)
                        vector.tensor_tensor(out=uz_buf[:],
                                             in0=dtx_sup[:, dblk * LS:(dblk + 1) * LS],
                                             in1=b_bc[s % 2][:], op=ALU.mult)
                        vector.tensor_tensor_scan(out=h_buf[gl % 2][:], data0=a_buf[g % 2][:],
                                                  data1=uz_buf[:],
                                                  initial=state[:, gl:gl + 1],
                                                  op0=ALU.mult, op1=ALU.add)
                        vector.tensor_tensor(out=uz_buf[:], in0=h_buf[gl % 2][:],
                                             in1=c_bc[s % 2][:], op=ALU.mult)
                        vector.tensor_tensor(out=y_sup[:, dblk * LS:(dblk + 1) * LS],
                                             in0=y_sup[:, dblk * LS:(dblk + 1) * LS],
                                             in1=uz_buf[:], op=ALU.add).then_inc(s_scan, 1)

            @block.tensor
            def _(tensor, c=c):
                def tr_prod(ob):
                    j = 8 * c + ob
                    tensor.wait_ge(s_prod, j + 1)
                    for q in range(NTT):
                        g = T0 + 64 * c + 32 + 4 * ob + q
                        tensor.wait_ge(s_trc, g - 1)
                        tensor.transpose(tr_bank(g)[:, 0:128],
                                         in_=prod[:, q * 128:(q + 1) * 128],
                                         identity=ident[:])
                        pe_fence(tensor, tr_bank(g), s_tr)

                tensor.wait_ge(s_scan, 256 * (c + 1))
                tensor.wait_ge(s_gch, 16 * (c + 1))
                tensor.wait_ge(s_hi, 16 * (c + 1))
                for ob in range(NOB):
                    j = 8 * c + ob
                    u_a = 48 * c + 32 + 2 * ob
                    tensor.wait_ge(s_prod, max(0, j - 1))
                    tensor.wait_ge(s_w, W0 + 16 * (u_a + 1))
                    slot_a = wpool[:, (u_a % 2) * 1024:(u_a % 2 + 1) * 1024]
                    slot_b = wpool[:, ((u_a + 1) % 2) * 1024:((u_a + 1) % 2 + 1) * 1024]
                    for dblk in range(8):
                        tensor.matmul(pt_out[ob % 2][:],
                                      lhsT=slot_a[:, dblk * 128:(dblk + 1) * 128],
                                      rhs=y_sup[:, dblk * LS:(dblk + 1) * LS],
                                      start=(dblk == 0), stop=False)
                    tensor.wait_ge(s_w, W0 + 16 * (u_a + 2))
                    for dblk in range(8, NDB):
                        tensor.matmul(pt_out[ob % 2][:],
                                      lhsT=slot_b[:, (dblk - 8) * 128:(dblk - 7) * 128],
                                      rhs=y_sup[:, dblk * LS:(dblk + 1) * LS],
                                      start=False, stop=(dblk == NDB - 1))
                    pe_fence(tensor, pt_out[ob % 2], s_wfree, k=2)
                    tensor.wait_ge(s_gate, j)
                    tensor.matmul(pt_xi[:], lhsT=ffn_wo[:, ob * 128:(ob + 1) * 128],
                                  rhs=g_fm[1][:], start=True, stop=True)
                    pe_fence(tensor, pt_xi, s_fpre)
                    if ob >= 1:
                        tr_prod(ob - 1)
                tr_prod(NOB - 1)

            @block.scalar
            def _(scalar, c=c):
                for ob in range(NOB):
                    j = 8 * c + ob
                    scalar.wait_ge(s_fpre, j + 1)
                    scalar.wait_ge(s_prod, max(0, j - 1))
                    scalar.activation(out=g_tile[ob % 2][:], in_=pt_xi[:], func=AF.Silu,
                                      bias=ffn_boc[:, ob:ob + 1]).then_inc(s_gate, 1)

            @block.vector
            def _(vector, c=c):
                for ob in range(NOB):
                    j = 8 * c + ob
                    vector.wait_ge(s_wfree, 48 * c + 32 + 2 * ob + 2)
                    vector.wait_ge(s_gate, j + 1)
                    if j >= 1:
                        vector.wait_ge(s_tr, tr_prod_end(j - 1))
                    vector.tensor_tensor(out=prod[:], in0=pt_out[ob % 2][:],
                                         in1=g_tile[ob % 2][:],
                                         op=ALU.mult).then_inc(s_prod, 1)
                    for q in range(NTT):
                        g = T0 + 64 * c + 32 + 4 * ob + q
                        jj = 32 * c + 4 * ob + q
                        vector.wait_ge(s_tr, g + 1)
                        if jj >= 2:
                            vector.wait_ge(s_ost, 16 * (jj - 1))
                        vector.tensor_tensor(
                            out=out_stg[jj % 2][:],
                            in0=tr_bank(g)[:, 0:128],
                            in1=x_tok[:, q * 1024 + ob * 128: q * 1024 + (ob + 1) * 128],
                            op=ALU.add).then_inc(s_trc, 1)

            @block.gpsimd
            def _(gpsimd, c=c, t0=t0):
                for ob in range(NOB):
                    for q in range(NTT):
                        j = 32 * c + 4 * ob + q
                        gpsimd.wait_ge(s_trc, T0 + 64 * c + 32 + 4 * ob + q + 1)
                        gpsimd.dma_start(out=out_hbm[t0 + q * 128: t0 + (q + 1) * 128,
                                                     ob * 128:(ob + 1) * 128],
                                         in_=out_stg[j % 2][:]).then_inc(s_ost, 16)

            if c + 1 < nsc:
                @block.sync
                def _(sync, c=c):
                    sync.wait_ge(s_trc, trc_all_end(c))
                    nt0 = (c + 1) * LS
                    for tt in range(NTT):
                        sync.dma_start(out=x_tok[:, tt * 1024:(tt + 1) * 1024],
                                       in_=x_hbm[nt0 + tt * 128: nt0 + (tt + 1) * 128, :]
                                       ).then_inc(s_x, 16)

        NOST = 16 * 32 * nsc

        @block.gpsimd
        def _(gpsimd):
            gpsimd.wait_ge(s_ost, NOST)

        if taps:
            @block.sync
            def _(sync):
                sync.wait_ge(s_ost, NOST)
                n = NOST
                for name, ap in taps.items():
                    srcap = {"xn_fm": xn_fm, "dt": dt_sup, "dtx": dtx_sup,
                             "y": y_sup, "bbc": b_bc[1], "state": state,
                             "ident": ident, "es": es_all, "xtok": x_tok,
                             "mrow": mneg_bc, "m2row": r_bc,
                             }[name]
                    sync.dma_start(out=ap, in_=srcap[:]).then_inc(s_ost, 16)
                    n += 16
                sync.wait_ge(s_ost, n)

    return nc


_NC_CACHE = {}


def kernel(L0_attn_x, L0_attn_x_proj_w, L0_attn_dt_proj_w, L0_attn_dt_proj_b,
           L0_attn_A_log, L0_attn_D, L0_attn_B_proj_w, L0_attn_C_proj_w,
           L0_attn_conv_w, L0_attn_conv_b, L0_attn_out_proj_w,
           L0_ffn_Win, L0_ffn_bin_, L0_ffn_W1, L0_ffn_b1, L0_ffn_W2,
           L0_ffn_b2, L0_ffn_Wout, L0_ffn_bout):
    x = np.ascontiguousarray(np.asarray(L0_attn_x, dtype=np.float32))
    w = {
        "x_proj_w": L0_attn_x_proj_w, "dt_proj_w": L0_attn_dt_proj_w,
        "dt_proj_b": L0_attn_dt_proj_b, "A_log": L0_attn_A_log,
        "D_vec": L0_attn_D, "B_proj_w": L0_attn_B_proj_w,
        "C_proj_w": L0_attn_C_proj_w,
        "conv_w": np.asarray(L0_attn_conv_w).reshape(DI, 4),
        "conv_b": L0_attn_conv_b, "out_proj_w": L0_attn_out_proj_w,
        "ffn_Win": L0_ffn_Win, "ffn_bin": L0_ffn_bin_, "ffn_W1": L0_ffn_W1,
        "ffn_b1": L0_ffn_b1, "ffn_W2": L0_ffn_W2, "ffn_b2": L0_ffn_b2,
        "ffn_Wout": L0_ffn_Wout, "ffn_bout": L0_ffn_bout,
    }
    w = {k: np.ascontiguousarray(np.asarray(v, dtype=np.float32)) for k, v in w.items()}
    if "nc" not in _NC_CACHE:
        _NC_CACHE["nc"] = build_nc()
    nc = _NC_CACHE["nc"]
    in_maps = [dict(w, x=np.ascontiguousarray(x[b])) for b in range(8)]
    res = run_bass_kernel_spmd(nc, in_maps, core_ids=list(range(8)))
    return np.stack([r["out"] for r in res.results], axis=0)
